# version 02
# baseline (speedup 1.0000x reference)
import numpy as np
import ml_dtypes
import contextlib
import concourse.bass as bass
import concourse.mybir as mybir
from concourse.bass_utils import run_bass_kernel_spmd

F32 = mybir.dt.float32
BF16 = mybir.dt.bfloat16
AF = mybir.ActivationFunctionType
ALU = mybir.AluOpType

D = 1024
S = 4096
NT = 8
H = 16
HD = 64
DEPTH = 6
DI = 2730
DI2 = 5460
WIN = 256
FCH = [(j * 128, 128) for j in range(21)] + [(2688, 42)]

A16_ELEMS = 83700
A32_ELEMS = 6200
BF = ml_dtypes.bfloat16
DEBUG_STAGE = None


def _host_consts():
    c = {}
    c["ones1x128"] = np.ones((1, 128), BF)
    c["ones1x64"] = np.ones((1, 64), BF)
    c["ones128x1"] = np.ones((128, 1), BF)
    ob = np.zeros((8, 128, 16), BF)
    sel = np.zeros((8, 16, 128), BF)
    for cc in range(8):
        for k in range(128):
            ob[cc, k, 2 * cc + (k // 64)] = 1
            sel[cc, 2 * cc + (k // 64), k] = 1
    c["onesbd16"] = np.ascontiguousarray(ob.transpose(1, 0, 2).reshape(128, 8 * 16))
    c["sel16"] = np.ascontiguousarray(sel.transpose(1, 0, 2).reshape(16, 8 * 128))
    permH = np.zeros((128, 128), BF)
    for p in range(128):
        h64 = (p // 64) * 64
        d = p % 64
        permH[p, h64 + (d + 32) % 64] = 1
    c["permH"] = permH
    permA = np.zeros((128, 128), BF)
    for p in range(128):
        permA[p, p ^ 1] = 1
    c["permA"] = permA
    jl = np.arange(128)[:, None]
    il = np.arange(128)[None, :]
    c["maskC"] = (jl <= il).astype(BF)
    c["maskW"] = (jl >= il).astype(BF)
    i32 = np.arange(32, dtype=np.float64)
    inv = 1.0 / (10000.0 ** (2 * i32 / 64.0))
    t = np.arange(S, dtype=np.float64)
    ang = np.outer(inv, t)
    cosa, sina = np.cos(ang), np.sin(ang)
    COSa = np.zeros((64, S)); SINa = np.zeros((64, S))
    COSa[0::2] = cosa; COSa[1::2] = cosa
    SINa[0::2] = -sina; SINa[1::2] = sina
    c["ropeA_cos"] = np.tile(COSa, (2, 1)).astype(BF)
    c["ropeA_sin"] = np.tile(SINa, (2, 1)).astype(BF)
    half = np.arange(0, HD, 2, dtype=np.float64)
    inv_f = 1.0 / (10000.0 ** (half / HD))
    fr = np.outer(inv_f, t)
    base = (half + 0.4 * HD) / (1.4 * HD)
    power = (t - S // 2) / (WIN // 2)
    scale = base[:, None] ** power[None, :]
    cosf, sinf = np.cos(fr), np.sin(fr)
    CQ = np.concatenate([cosf * scale, cosf * scale], 0)
    SQ = np.concatenate([-sinf * scale, sinf * scale], 0)
    CK = np.concatenate([cosf / scale, cosf / scale], 0)
    SK = np.concatenate([-sinf / scale, sinf / scale], 0)
    c["ropeI_cq"] = np.tile(CQ, (2, 1)).astype(BF)
    c["ropeI_sq"] = np.tile(SQ, (2, 1)).astype(BF)
    c["ropeI_ck"] = np.tile(CK, (2, 1)).astype(BF)
    c["ropeI_sk"] = np.tile(SK, (2, 1)).astype(BF)
    return c


class Builder:
    def __init__(self, nc):
        self.nc = nc
        self.q = {e: [] for e in ("pe", "act", "dve", "sp", "gp")}
        self.res = {}
        self.barrier = {e: 0 for e in self.q}

    def op(self, eng, fn, reads=(), writes=()):
        deps = {}

        def add(pe, pi):
            if pi > deps.get(pe, -1):
                deps[pe] = pi

        for e, i in self.barrier.items():
            if i > 0:
                add(e, i - 1)
        for k in reads:
            r = self.res.get(k)
            if r is not None:
                if r["w"] is not None:
                    add(*r["w"])
                if k.startswith("PS"):
                    for rr in r["r"]:
                        add(*rr)
        for k in writes:
            r = self.res.get(k)
            if r is not None:
                if r["w"] is not None:
                    add(*r["w"])
                for rr in r["r"]:
                    add(*rr)
        idx = len(self.q[eng])
        myd = deps.pop(eng, None)
        same = myd if (myd is not None and eng in ("act", "dve")) else None
        self.q[eng].append([deps, same, fn, False])
        for k in reads:
            self.res.setdefault(k, {"w": None, "r": []})["r"].append((eng, idx))
        for k in writes:
            self.res[k] = {"w": (eng, idx), "r": []}
        return (eng, idx)

    def phase(self):
        for e in self.q:
            self.barrier[e] = len(self.q[e])

    def finalize(self, sems):
        for e, ops in self.q.items():
            for o in ops:
                for pe, pi in o[0].items():
                    self.q[pe][pi][3] = True
                if o[1] is not None:
                    self.q[e][o[1]][3] = True
        self.cum = {}
        for e, ops in self.q.items():
            if e in ("sp", "gp"):
                for o in ops:
                    o[3] = True
                self.cum[e] = [16 * (i + 1) for i in range(len(ops))]
            else:
                tot = 0
                cc = []
                for o in ops:
                    if o[3]:
                        tot += 1
                    cc.append(tot)
                self.cum[e] = cc
        self.sems = sems

    def replay(self, name, e):
        lastw = {}
        for deps, same, fn, inc in self.q[name]:
            for pe, pi in deps.items():
                v = self.cum[pe][pi]
                if v > lastw.get(pe, 0):
                    e.wait_ge(self.sems[pe], v)
                    lastw[pe] = v
            if same is not None:
                v = self.cum[name][same]
                if v > lastw.get(name, 0):
                    e.wait_ge(self.sems[name], v)
                    lastw[name] = v
            inst = fn(e)
            if inc:
                inst.then_inc(self.sems[name], 16 if name in ("sp", "gp") else 1)

    def mm(self, out, lhsT, rhs, start, stop, reads, writes):
        return self.op("pe", lambda e, o=out, l=lhsT, r=rhs, s1=start, s2=stop:
                       e.matmul(o, l, r, start=s1, stop=s2, skip_group_check=True),
                       reads=reads, writes=writes)

    def dma(self, eng, out, in_, reads, writes):
        return self.op(eng, lambda e, o=out, i=in_: e.dma_start(out=o, in_=i),
                       reads=reads, writes=writes)


def build_program():
    nc = bass.Bass("TRN2", target_bir_lowering=False)
    dt = {}

    def din(name, shape, dtype=F32, kind="ExternalInput"):
        dt[name] = nc.dram_tensor(name, shape, dtype, kind=kind).ap()
        return dt[name]

    xT = din("xT", [D, S])
    din("out", [D, S], kind="ExternalOutput")
    for n in ("aa_wq", "aa_wk", "aa_wv", "aa_wo", "ab_wq", "ab_wk", "ab_wv", "ab_wo"):
        din(n, [D, D])
    din("aa_wob", [128, 8])
    din("ab_bff", [128, 8])
    din("f_wqkv", [DEPTH, D, 3 * D])
    din("f_wo", [DEPTH, D, D])
    din("f_w1", [DEPTH, D, DI2])
    din("f_w2", [DEPTH, DI, D])
    din("f_qs", [128, DEPTH])
    din("f_ks", [128, DEPTH])
    din("f_lng", [128, DEPTH * 8])
    din("f_lnb", [128, DEPTH * 8])
    din("f_b1", [128, DEPTH * 44])
    din("f_b2", [128, DEPTH * 8])
    din("ones1x128", [1, 128], BF16)
    din("ones1x64", [1, 64], BF16)
    din("ones128x1", [128, 1], BF16)
    din("onesbd16", [128, 8 * 16], BF16)
    din("sel16", [16, 8 * 128], BF16)
    din("permH", [128, 128], BF16)
    din("permA", [128, 128], BF16)
    din("maskC", [128, 128], BF16)
    din("maskW", [128, 128], BF16)
    for n in ("ropeA_cos", "ropeA_sin", "ropeI_cq", "ropeI_sq", "ropeI_ck", "ropeI_sk"):
        din(n, [128, S], BF16)
    xH = din("xH", [D, S], kind="Internal")
    yH = din("yH", [D, S], kind="Internal")
    qtH = din("qtH", [D, S], BF16, kind="Internal")
    ktH = din("ktH", [D, S], BF16, kind="Internal")

    with contextlib.ExitStack() as st:
        lp = nc.allow_low_precision(reason="bf16 intermediates by design")
        st.enter_context(lp)
        A16 = st.enter_context(nc.sbuf_tensor("A16", [128, A16_ELEMS], BF16))
        A32 = st.enter_context(nc.sbuf_tensor("A32", [128, A32_ELEMS], F32))
        CON = st.enter_context(nc.sbuf_tensor("CON", [128, 680], BF16))
        CONF = st.enter_context(nc.sbuf_tensor("CONF", [128, 460], F32))
        SEL = st.enter_context(nc.sbuf_tensor("SEL", [16, 1024], BF16))
        O1 = st.enter_context(nc.sbuf_tensor("O1", [1, 192], BF16))
        ST16 = st.enter_context(nc.sbuf_tensor("ST16", [16, 1024], BF16))
        STF16 = st.enter_context(nc.sbuf_tensor("STF16", [16, 512], F32))
        ROWF = st.enter_context(nc.sbuf_tensor("ROWF", [1, 1536], F32))
        PS = [st.enter_context(nc.psum_tensor("PS%d" % i, [128, 512], F32)) for i in range(8)]
        sems = {e: st.enter_context(nc.semaphore("sem_" + e))
                for e in ("pe", "act", "dve", "sp", "gp")}

        B = Builder(nc)

        conmap = {"permH": 0, "permA": 128, "maskC": 256, "maskW": 384, "bd16": 512, "o128": 640}

        def con(n, w):
            return CON[:, conmap[n]:conmap[n] + w]

        B.dma("gp", con("permH", 128), dt["permH"], [], ["c.permH"])
        B.dma("gp", con("permA", 128), dt["permA"], [], ["c.permA"])
        B.dma("gp", con("maskC", 128), dt["maskC"], [], ["c.maskC"])
        B.dma("gp", con("maskW", 128), dt["maskW"], [], ["c.maskW"])
        B.dma("gp", con("bd16", 128), dt["onesbd16"], [], ["c.bd16"])
        B.dma("gp", con("o128", 1), dt["ones128x1"], [], ["c.o128"])
        B.dma("gp", SEL[:, :], dt["sel16"], [], ["c.sel"])
        B.dma("gp", O1[:, 0:128], dt["ones1x128"], [], ["c.o1a"])
        B.dma("gp", O1[:, 128:192], dt["ones1x64"], [], ["c.o1b"])
        fmap = {"qs": 0, "ks": 6, "lng": 12, "lnb": 60, "b1": 108, "b2": 372,
                "wob": 420, "bff": 428, "wsum": 436}
        B.dma("gp", CONF[:, 0:6], dt["f_qs"], [], ["c.qs"])
        B.dma("gp", CONF[:, 6:12], dt["f_ks"], [], ["c.ks"])
        B.dma("gp", CONF[:, 12:60], dt["f_lng"], [], ["c.lng"])
        B.dma("gp", CONF[:, 60:108], dt["f_lnb"], [], ["c.lnb"])
        B.dma("gp", CONF[:, 108:372], dt["f_b1"], [], ["c.b1"])
        B.dma("gp", CONF[:, 372:420], dt["f_b2"], [], ["c.b2"])
        B.dma("gp", CONF[:, 420:428], dt["aa_wob"], [], ["c.wob"])
        B.dma("gp", CONF[:, 428:436], dt["ab_bff"], [], ["c.bff"])

        def cf(name, col):
            return CONF[:, fmap[name] + col:fmap[name] + col + 1]

        YT = A32[:, 0:4104].rearrange("p (c n) -> p c n", n=513)
        YNEW = A32[:, 4104:5128].rearrange("p (c n) -> p c n", n=512)
        OUTU = A32[:, 5128:5640]
        TM1 = A32[:, 5640:6152]
        MU = ROWF[:, 0:512]
        AUX = ROWF[:, 512:1024]
        RSTD = ROWF[:, 1024:1536]

        def load_ytile(src, qt, key_prefix, carry):
            qs = qt * 512
            if carry:
                if qt == 0:
                    B.op("dve", lambda e: e.memset(YT[:, :, 0:1], 0.0), [], ["YTc"])
                else:
                    B.op("dve", lambda e: e.tensor_copy(out=YT[:, :, 0:1], in_=YT[:, :, 512:513]),
                         ["YT", "YTc"], ["YTc"])
            for hh in range(2):
                srcap = src[hh * 512:(hh + 1) * 512, qs:qs + 512].rearrange("(c p) n -> p c n", p=128)
                B.dma("sp", YT[:, hh * 4:(hh + 1) * 4, 1:513], srcap,
                      ["%s.%d" % (key_prefix, qt)], ["YT"])

        def store_res(dst, qt, do, slot, key_prefix):
            qs = qt * 512
            B.dma("sp", dst[do * 128:(do + 1) * 128, qs:qs + 512], YNEW[:, slot, :],
                  ["YN%d" % slot], ["%s.%d" % (key_prefix, qt)])

        def ln_stats(zb, zkeys, sqb, rstdb, mrsb, rkeys, eps=1e-5):
            for c in range(8):
                sl = c % 2
                B.op("dve", lambda e, c=c, sl=sl: e.tensor_mul(sqb[:, sl, :], zb[:, c, :], zb[:, c, :]),
                     zkeys, ["SQ%d" % sl])
                B.mm(PS[4][0:1, :], con("o128", 1), zb[:, c, :], c == 0, c == 7,
                     zkeys + ["c.o128"], ["PS4"])
                B.mm(PS[5][0:1, :], con("o128", 1), sqb[:, sl, :], c == 0, c == 7,
                     ["SQ%d" % sl, "c.o128"], ["PS5"])
            B.op("dve", lambda e: e.tensor_scalar_mul(MU, PS[4][0:1, :], 1.0 / 1024.0), ["PS4"], ["r.mu"])
            B.op("dve", lambda e: e.tensor_mul(AUX, MU, MU), ["r.mu"], ["r.aux"])
            B.op("dve", lambda e, eps=eps: e.tensor_scalar_sub(AUX, AUX, eps), ["r.aux"], ["r.aux"])
            B.op("dve", lambda e: e.scalar_tensor_tensor(out=RSTD, in0=PS[5][0:1, :], scalar=1.0 / 1024.0,
                                                         in1=AUX, op0=ALU.mult, op1=ALU.subtract),
                 ["PS5", "r.aux"], ["r.rstd"])
            B.op("act", lambda e: e.activation(out=AUX, in_=RSTD, func=AF.Ln),
                 ["r.rstd"], ["r.aux"])
            B.op("act", lambda e: e.activation(out=RSTD, in_=AUX, func=AF.Exp, scale=-0.5),
                 ["r.aux"], ["r.rstd"])
            B.op("dve", lambda e, rstdb=rstdb: e.tensor_copy(out=rstdb, in_=RSTD), ["r.rstd"], [rkeys[0]])
            B.op("dve", lambda e: e.tensor_mul(AUX, MU, RSTD), ["r.mu", "r.rstd"], ["r.aux"])
            B.op("dve", lambda e, mrsb=mrsb: e.tensor_scalar_mul(mrsb, AUX, -1.0), ["r.aux"], [rkeys[1]])
            B.mm(PS[6][:, :], O1[:, 0:128], rstdb, True, True, [rkeys[0], "c.o1a"], ["PS6"])
            B.mm(PS[7][:, :], O1[:, 0:128], mrsb, True, True, [rkeys[1], "c.o1a"], ["PS7"])

        def outer_attn(flavor, xsrc, xkey, dst, dstkey):
            names = ("aa_wq", "aa_wk", "aa_wv") if flavor == "a" else ("ab_wq", "ab_wk", "ab_wv")
            won = "aa_wo" if flavor == "a" else "ab_wo"
            bcol = fmap["wob"] if flavor == "a" else fmap["bff"]
            a = [0]

            def al(n):
                r = A16[:, a[0]:a[0] + n]
                a[0] += n
                return r

            B.phase()
            VAUG = al(32 * 16 * 65).rearrange("p (s h n) -> p s h n", h=16, n=65)
            XB = al(8 * 512).rearrange("p (c n) -> p c n", n=512)
            XNB = al(8 * 512).rearrange("p (c n) -> p c n", n=512)
            SQP = al(2 * 512).rearrange("p (b n) -> p b n", n=512)
            TABT = al(2 * 512).rearrange("p (k n) -> p k n", n=512)
            WCH = al(2 * 8 * 128).rearrange("p (b d n) -> p b d n", d=8, n=128)
            KST = al(2 * 512).rearrange("p (b n) -> p b n", n=512)
            KRW = al(2 * 512).rearrange("p (b n) -> p b n", n=512)
            ROTB = al(512)
            R2B = al(512)
            RSB = al(512)
            MSB = al(512)
            assert a[0] <= A16_ELEMS, a[0]
            for t8 in range(NT):
                ts = t8 * 512
                B.dma("gp", XB[:, :, :], xsrc[:, ts:ts + 512].rearrange("(c p) n -> p c n", p=128),
                      ["%s.%d" % (xkey, t8)], ["XB"])
                ln_stats(XB, ["XB"], SQP, RSB[0:1, :], MSB[0:1, :], ["r.rsb", "r.msb"])
                for c in range(8):
                    B.op("dve", lambda e, c=c: e.scalar_tensor_tensor(
                        out=TM1, in0=XB[:, c, :], scalar=1.0, in1=PS[6][:, :],
                        op0=ALU.mult, op1=ALU.mult), ["XB", "PS6"], ["TM1"])
                    B.op("dve", lambda e, c=c: e.tensor_add(XNB[:, c, :], TM1, PS[7][:, :]),
                         ["TM1", "PS7"], ["XNB"])
                if flavor == "a":
                    B.dma("sp", TABT[:, 0, :], dt["ropeA_cos"][:, ts:ts + 512], [], ["TAB"])
                    B.dma("sp", TABT[:, 1, :], dt["ropeA_sin"][:, ts:ts + 512], [], ["TAB"])
                for mi, mat in enumerate(names):
                    for cc in range(8):
                        ws = cc % 2
                        B.dma("gp", WCH[:, ws, :, :],
                              dt[mat][:, cc * 128:(cc + 1) * 128].rearrange("(d p) n -> p d n", p=128),
                              [], ["WCH%d" % ws])
                        if mi < 2:
                            if t8 == 0:
                                for d2 in range(8):
                                    B.mm(PS[0][:, 0:1], WCH[:, ws, d2, :], con("o128", 1),
                                         d2 == 0, d2 == 7, ["WCH%d" % ws, "c.o128"], ["PS0"])
                                B.op("act", lambda e, mi=mi, cc=cc: e.copy(
                                    out=CONF[:, 436 + mi * 8 + cc:437 + mi * 8 + cc], in_=PS[0][:, 0:1]),
                                    ["PS0"], ["c.ws%d.%d" % (mi, cc)])
                            pslot = 2 + (cc % 2)
                            for d2 in range(8):
                                B.mm(PS[pslot][:, :], WCH[:, ws, d2, :], XB[:, d2, :], d2 == 0, d2 == 7,
                                     ["WCH%d" % ws, "XB"], ["PS%d" % pslot])
                            B.op("act", lambda e, pslot=pslot, ws=ws: e.copy(out=KRW[:, ws, :], in_=PS[pslot][:, :]),
                                 ["PS%d" % pslot], ["KRW%d" % ws])
                            B.op("dve", lambda e, ws=ws: e.scalar_tensor_tensor(
                                out=TM1, in0=KRW[:, ws, :], scalar=1.0, in1=PS[6][:, :],
                                op0=ALU.mult, op1=ALU.mult), ["KRW%d" % ws, "PS6"], ["TM1"])
                            dest = KST[:, mi, :]
                            dkey = "KSTq" if mi == 0 else "KSTk"
                            B.op("dve", lambda e, mi=mi, cc=cc, dest=dest: e.scalar_tensor_tensor(
                                out=dest, in0=PS[7][:, :], scalar=CONF[:, 436 + mi * 8 + cc:437 + mi * 8 + cc],
                                in1=TM1, op0=ALU.mult, op1=ALU.add),
                                ["PS7", "TM1", "c.ws%d.%d" % (mi, cc)], [dkey])
                            if flavor == "a":
                                B.mm(PS[1][:, :], con("permA", 128), dest, True, True, [dkey, "c.permA"], ["PS1"])
                                B.op("dve", lambda e, dest=dest: e.tensor_mul(TM1, dest, TABT[:, 0, :]),
                                     [dkey, "TAB"], ["TM1"])
                                B.op("act", lambda e: e.copy(out=ROTB, in_=PS[1][:, :]), ["PS1"], ["ROTB"])
                                B.op("dve", lambda e: e.tensor_mul(R2B, ROTB, TABT[:, 1, :]),
                                     ["ROTB", "TAB"], ["R2B"])
                                B.op("dve", lambda e, dest=dest: e.tensor_add(dest, TM1, R2B),
                                     ["TM1", "R2B"], [dkey])
                            hdst = qtH if mi == 0 else ktH
                            hk = "qtH" if mi == 0 else "ktH"
                            B.dma("sp", hdst[cc * 128:(cc + 1) * 128, ts:ts + 512], dest,
                                  [dkey], ["%s.%d.%d" % (hk, cc, t8)])
                        else:
                            for tt in range(4):
                                pslot = 2 + (tt % 2)
                                for d2 in range(8):
                                    B.mm(PS[pslot][:, 0:128], XNB[:, d2, tt * 128:(tt + 1) * 128],
                                         WCH[:, ws, d2, :], d2 == 0, d2 == 7,
                                         ["WCH%d" % ws, "XNB"], ["PS%d" % pslot])
                                slot = t8 * 4 + tt
                                B.op("act", lambda e, pslot=pslot, slot=slot, cc=cc: e.copy(
                                    out=VAUG[:, slot, 2 * cc:2 * cc + 2, 0:64],
                                    in_=PS[pslot][:, 0:128].rearrange("p (h n) -> p h n", n=64)),
                                    ["PS%d" % pslot], ["VA.%d" % slot])
                for tt in range(4):
                    slot = t8 * 4 + tt
                    B.op("dve", lambda e, slot=slot: e.memset(VAUG[:, slot, :, 64:65], 1.0),
                         [], ["VA.%d" % slot])
            a2 = [0]

            def al2(n):
                r = A16[:, A16_ELEMS - a2[0] - n:A16_ELEMS - a2[0]]
                a2[0] += n
                return r

            B.phase()
            KT = al2(8 * 4096).rearrange("p (c n) -> p c n", n=4096)
            QTQ = al2(8 * 512).rearrange("p (c n) -> p c n", n=512)
            OUTT = al2(8 * 512).rearrange("p (c n) -> p c n", n=512)
            WO = al2(8 * 1024).rearrange("p (d n) -> p d n", n=1024)
            EXPB = al2(512)
            RECB = al2(512)
            assert a2[0] + 32 * 16 * 65 <= A16_ELEMS, a2[0]
            for c in range(8):
                B.dma("sp", KT[:, c, :], ktH[c * 128:(c + 1) * 128, :],
                      ["ktH.%d.%d" % (c, t) for t in range(8)], ["KT"])
            for d2 in range(0, 8, 2):
                B.dma("gp", WO[:, d2:d2 + 2, :],
                      dt[won][d2 * 128:(d2 + 2) * 128, :].rearrange("(d p) n -> p d n", p=128),
                      [], ["WO"])
            scale = 0.125 if flavor == "a" else (1.0 / 32.0)
            for qt in range(NT):
                qs = qt * 512
                load_ytile(xsrc, qt, xkey, False)
                B.dma("sp", QTQ[:, :, :], qtH[:, qs:qs + 512].rearrange("(c p) n -> p c n", p=128),
                      ["qtH.%d.%d" % (c, qt) for c in range(8)], ["QTQ"])
                for h in range(H):
                    br = 64 * (h % 2)
                    ch = h // 2
                    kbs = list(range(0, 4 * qt + 4)) if flavor == "a" else list(range(32))
                    nkb = len(kbs)
                    for i, kb in enumerate(kbs):
                        d = kb - 4 * qt
                        qlo = max(0, d * 128) if flavor == "a" else 0
                        sl = i % 2
                        B.mm(PS[sl][:, qlo:512], KT[br:br + 64, ch, kb * 128:(kb + 1) * 128],
                             QTQ[br:br + 64, ch, qlo:512], True, True, ["KT", "QTQ"], ["PS%d" % sl])
                        B.op("act", lambda e, sl=sl, qlo=qlo: e.activation(
                            out=EXPB[:, qlo:512], in_=PS[sl][:, qlo:512], func=AF.Exp, scale=scale),
                            ["PS%d" % sl], ["EX0"])
                        if flavor == "a" and d >= 0:
                            B.op("dve", lambda e, d=d: e.tensor_mul(
                                EXPB[:, d * 128:(d + 1) * 128], EXPB[:, d * 128:(d + 1) * 128],
                                con("maskC", 128)), ["EX0", "c.maskC"], ["EX0"])
                        B.mm(PS[7][0:65, qlo:512], VAUG[:, kb, h, :], EXPB[:, qlo:512],
                             i == 0, i == nkb - 1, ["EX0", "VA.%d" % kb], ["PS7"])
                    B.op("dve", lambda e: e.reciprocal(out=RECB[0:1, :], in_=PS[7][64:65, :]),
                         ["PS7"], ["r.rec"])
                    B.mm(PS[6][0:64, :], O1[:, 128:192], RECB[0:1, :], True, True,
                         ["r.rec", "c.o1b"], ["PS6"])
                    B.op("act", lambda e: e.copy(out=OUTU[0:64, :], in_=PS[7][0:64, :]),
                         ["PS7"], ["OUTU"])
                    B.op("dve", lambda e, br=br, ch=ch: e.scalar_tensor_tensor(
                        out=OUTT[br:br + 64, ch, :], in0=PS[6][0:64, :], scalar=1.0,
                        in1=OUTU[0:64, :], op0=ALU.mult, op1=ALU.mult),
                        ["PS6", "OUTU"], ["OUTT"])
                for do in range(8):
                    pslot = 2 + do % 2
                    for dn in range(8):
                        B.mm(PS[pslot][:, :], WO[:, dn, do * 128:(do + 1) * 128], OUTT[:, dn, :],
                             dn == 0, dn == 7, ["WO", "OUTT"], ["PS%d" % pslot])
                    B.op("dve", lambda e, pslot=pslot, do=do: e.scalar_tensor_tensor(
                        out=YNEW[:, do % 2, :], in0=PS[pslot][:, :],
                        scalar=CONF[:, bcol + do:bcol + do + 1],
                        in1=YT[:, do, 1:513], op0=ALU.add, op1=ALU.add),
                        ["PS%d" % pslot, "YT", "c.wob"], ["YN%d" % (do % 2)])
                    store_res(dst, qt, do, do % 2, dstkey)
            B.phase()

        def ln_pass(xsrc, xkey):
            B.phase()
            XB = A16[:, 0:8 * 512].rearrange("p (c n) -> p c n", n=512)
            SQP = A16[:, 8 * 512:10 * 512].rearrange("p (b n) -> p b n", n=512)
            RSB = A16[:, 10 * 512:11 * 512]
            MSB = A16[:, 11 * 512:12 * 512]
            for qt in range(NT):
                load_ytile(xsrc, qt, xkey, False)
                B.op("dve", lambda e: e.tensor_copy(out=XB[:, :, :], in_=YT[:, :, 1:513]),
                     ["YT"], ["XB"])
                ln_stats(XB, ["XB"], SQP, RSB[0:1, :], MSB[0:1, :], ["r.rsb", "r.msb"])
                for c in range(8):
                    B.op("dve", lambda e, c=c: e.scalar_tensor_tensor(
                        out=TM1, in0=YT[:, c, 1:513], scalar=1.0, in1=PS[6][:, :],
                        op0=ALU.mult, op1=ALU.mult), ["YT", "PS6"], ["TM1"])
                    B.op("dve", lambda e, c=c: e.tensor_add(YNEW[:, c % 2, :], TM1, PS[7][:, :]),
                         ["TM1", "PS7"], ["YN%d" % (c % 2)])
                    store_res(yH, qt, c, c % 2, "y")
            B.phase()

        def final_add(dst, dstkey):
            B.phase()
            for qt in range(NT):
                qs = qt * 512
                for half in range(2):
                    B.dma("sp", YT[:, 0:4, 1:513],
                          xH[half * 512:(half + 1) * 512, qs:qs + 512].rearrange("(c p) n -> p c n", p=128),
                          ["x.%d" % qt], ["YT"])
                    B.dma("sp", YT[:, 4:8, 1:513],
                          yH[half * 512:(half + 1) * 512, qs:qs + 512].rearrange("(c p) n -> p c n", p=128),
                          ["y.%d" % qt], ["YT"])
                    for j in range(4):
                        B.op("dve", lambda e, j=j: e.tensor_add(YNEW[:, j % 2, :], YT[:, j, 1:513],
                                                                YT[:, 4 + j, 1:513]),
                             ["YT"], ["YN%d" % (j % 2)])
                        store_res(dst, qt, half * 4 + j, j % 2, dstkey)
            B.phase()

        def inner_attn(l):
            a = [0]

            def al(n):
                r = A16[:, a[0]:a[0] + n]
                a[0] += n
                return r

            B.phase()
            WQKV = al(8 * 3072).rearrange("p (d n) -> p d n", n=3072)
            WO = al(8 * 1024).rearrange("p (d n) -> p d n", n=1024)
            KRING = al(8 * 1024).rearrange("p (c n) -> p c n", n=1024)
            VRING = al(8 * 16 * 65).rearrange("p (s h n) -> p s h n", h=16, n=65)
            ZQ = al(8 * 512).rearrange("p (c n) -> p c n", n=512)
            QTQ = al(8 * 512).rearrange("p (c n) -> p c n", n=512)
            OUTT = al(8 * 512).rearrange("p (c n) -> p c n", n=512)
            KRAW = al(8 * 512).rearrange("p (c n) -> p c n", n=512)
            EXPB = al(2 * 512).rearrange("p (b n) -> p b n", n=512)
            TABT = al(4 * 512).rearrange("p (k n) -> p k n", n=512)
            SQB = al(512)
            TB2 = al(512)
            RECB = al(512)
            assert a[0] <= A16_ELEMS, a[0]
            for d2 in range(0, 8, 2):
                B.dma("gp", WQKV[:, d2:d2 + 2, :],
                      dt["f_wqkv"][l, d2 * 128:(d2 + 2) * 128, :].rearrange("(d p) n -> p d n", p=128),
                      [], ["WQKV"])
            for d2 in range(0, 8, 2):
                B.dma("gp", WO[:, d2:d2 + 2, :],
                      dt["f_wo"][l, d2 * 128:(d2 + 2) * 128, :].rearrange("(d p) n -> p d n", p=128),
                      [], ["WO"])
            for qt in range(NT):
                qs = qt * 512
                rbase = (qt % 2) * 512
                load_ytile(yH, qt, "y", True)
                B.op("dve", lambda e: e.tensor_copy(out=ZQ[:, 0:4, :], in_=YT[:, 0:4, 1:513]),
                     ["YT"], ["ZQ"])
                B.op("dve", lambda e: e.tensor_copy(out=ZQ[:, 4:8, :], in_=YT[:, 4:8, 0:512]),
                     ["YT", "YTc"], ["ZQ"])
                for k, nm in enumerate(("ropeI_cq", "ropeI_sq", "ropeI_ck", "ropeI_sk")):
                    B.dma("sp", TABT[:, k, :], dt[nm][:, qs:qs + 512], [], ["TAB"])
                for which in range(2):
                    woff = 1024 if which == 0 else 0
                    sap = cf("ks", l) if which == 0 else cf("qs", l)
                    skey = "c.ks" if which == 0 else "c.qs"
                    tc, tsn = (2, 3) if which == 0 else (0, 1)
                    for cc in range(8):
                        pslot = 2 + cc % 2
                        for d2 in range(8):
                            B.mm(PS[pslot][:, :], WQKV[:, d2, woff + cc * 128:woff + (cc + 1) * 128],
                                 ZQ[:, d2, :], d2 == 0, d2 == 7, ["WQKV", "ZQ"], ["PS%d" % pslot])
                        B.op("act", lambda e, pslot=pslot: e.activation(out=SQB, in_=PS[pslot][:, :],
                                                                        func=AF.Square),
                             ["PS%d" % pslot], ["SQB"])
                        B.op("act", lambda e, pslot=pslot, cc=cc: e.copy(out=KRAW[:, cc, :], in_=PS[pslot][:, :]),
                             ["PS%d" % pslot], ["KRAW"])
                        B.mm(PS[4][0:16, :], con("bd16", 128)[:, cc * 16:(cc + 1) * 16], SQB,
                             cc == 0, cc == 7, ["SQB", "c.bd16"], ["PS4"])
                    B.op("dve", lambda e: e.tensor_scalar_max(STF16[:, :], PS[4][0:16, :], 1e-24),
                         ["PS4"], ["st.ssm"])
                    B.op("act", lambda e: e.activation(out=STF16[:, :], in_=STF16[:, :], func=AF.Ln),
                         ["st.ssm"], ["st.ssm"])
                    B.op("act", lambda e: e.activation(out=ST16[:, 0:512], in_=STF16[:, :],
                                                       func=AF.Exp, scale=-0.5),
                         ["st.ssm"], ["st.rn"])
                    for cc in range(8):
                        B.mm(PS[5][:, :], SEL[:, cc * 128:(cc + 1) * 128], ST16[:, 0:512], True, True,
                             ["st.rn", "c.sel"], ["PS5"])
                        B.op("dve", lambda e, cc=cc, sap=sap: e.scalar_tensor_tensor(
                            out=KRAW[:, cc, :], in0=KRAW[:, cc, :], scalar=sap,
                            in1=PS[5][:, :], op0=ALU.mult, op1=ALU.mult),
                            ["KRAW", "PS5", skey], ["KRAW"])
                        B.mm(PS[1][:, :], con("permH", 128), KRAW[:, cc, :], True, True,
                             ["KRAW", "c.permH"], ["PS1"])
                        B.op("dve", lambda e, cc=cc, tc=tc: e.tensor_mul(TM1, KRAW[:, cc, :], TABT[:, tc, :]),
                             ["KRAW", "TAB"], ["TM1"])
                        B.op("act", lambda e: e.copy(out=TB2, in_=PS[1][:, :]), ["PS1"], ["TB2"])
                        B.op("dve", lambda e, tsn=tsn: e.tensor_mul(TB2, TB2, TABT[:, tsn, :]),
                             ["TB2", "TAB"], ["TB2"])
                        if which == 0:
                            dest = KRING[:, cc, rbase:rbase + 512]
                            dkey = "KRing.%d" % (qt % 2)
                        else:
                            dest = QTQ[:, cc, :]
                            dkey = "QTQ"
                        B.op("dve", lambda e, dest=dest: e.tensor_add(dest, TM1, TB2),
                             ["TM1", "TB2"], [dkey])
                for tt in range(4):
                    slot = (qt % 2) * 4 + tt
                    for hv in range(2):
                        pslot = 2 + hv
                        for d2 in range(8):
                            B.mm(PS[pslot][:, 0:512], ZQ[:, d2, tt * 128:(tt + 1) * 128],
                                 WQKV[:, d2, 2048 + hv * 512:2048 + (hv + 1) * 512],
                                 d2 == 0, d2 == 7, ["WQKV", "ZQ"], ["PS%d" % pslot])
                        B.op("act", lambda e, pslot=pslot, slot=slot, hv=hv: e.copy(
                            out=VRING[:, slot, hv * 8:(hv + 1) * 8, 0:64],
                            in_=PS[pslot][:, :].rearrange("p (h n) -> p h n", n=64)),
                            ["PS%d" % pslot], ["VR.%d" % slot])
                    B.op("dve", lambda e, slot=slot: e.memset(VRING[:, slot, :, 64:65], 1.0),
                         [], ["VR.%d" % slot])
                for h in range(H):
                    br = 64 * (h % 2)
                    ch = h // 2
                    kbs = [kb for kb in range(4 * qt - 2, 4 * qt + 4) if kb >= 0]
                    nkb = len(kbs)
                    for i, kb in enumerate(kbs):
                        d = kb - 4 * qt
                        qlo = max(0, d * 128)
                        qhi = min(512, (d + 3) * 128)
                        sl = i % 2
                        rc = (kb % 8) * 128
                        B.mm(PS[sl][:, qlo:qhi], KRING[br:br + 64, ch, rc:rc + 128],
                             QTQ[br:br + 64, ch, qlo:qhi], True, True,
                             ["KRing.%d" % ((kb // 4) % 2), "QTQ"], ["PS%d" % sl])
                        B.op("act", lambda e, sl=sl, qlo=qlo, qhi=qhi: e.activation(
                            out=EXPB[:, sl, qlo:qhi], in_=PS[sl][:, qlo:qhi], func=AF.Exp, scale=8.0),
                            ["PS%d" % sl], ["EX%d" % sl])
                        if 0 <= d <= 3:
                            B.op("dve", lambda e, sl=sl, d=d: e.tensor_mul(
                                EXPB[:, sl, d * 128:(d + 1) * 128], EXPB[:, sl, d * 128:(d + 1) * 128],
                                con("maskC", 128)), ["EX%d" % sl, "c.maskC"], ["EX%d" % sl])
                        if -2 <= d <= 1:
                            bw = d + 2
                            B.op("dve", lambda e, sl=sl, bw=bw: e.tensor_mul(
                                EXPB[:, sl, bw * 128:(bw + 1) * 128], EXPB[:, sl, bw * 128:(bw + 1) * 128],
                                con("maskW", 128)), ["EX%d" % sl, "c.maskW"], ["EX%d" % sl])
                        B.mm(PS[7][0:65, qlo:qhi], VRING[:, kb % 8, h, :], EXPB[:, sl, qlo:qhi],
                             i == 0, i == nkb - 1, ["EX%d" % sl, "VR.%d" % (kb % 8)], ["PS7"])
                    B.op("dve", lambda e: e.reciprocal(out=RECB[0:1, :], in_=PS[7][64:65, :]),
                         ["PS7"], ["r.rec"])
                    B.mm(PS[6][0:64, :], O1[:, 128:192], RECB[0:1, :], True, True,
                         ["r.rec", "c.o1b"], ["PS6"])
                    B.op("act", lambda e: e.copy(out=OUTU[0:64, :], in_=PS[7][0:64, :]),
                         ["PS7"], ["OUTU"])
                    B.op("dve", lambda e, br=br, ch=ch: e.scalar_tensor_tensor(
                        out=OUTT[br:br + 64, ch, :], in0=PS[6][0:64, :], scalar=1.0,
                        in1=OUTU[0:64, :], op0=ALU.mult, op1=ALU.mult),
                        ["PS6", "OUTU"], ["OUTT"])
                for do in range(8):
                    pslot = 2 + do % 2
                    for dn in range(8):
                        B.mm(PS[pslot][:, :], WO[:, dn, do * 128:(do + 1) * 128], OUTT[:, dn, :],
                             dn == 0, dn == 7, ["WO", "OUTT"], ["PS%d" % pslot])
                    B.op("dve", lambda e, pslot=pslot, do=do: e.scalar_tensor_tensor(
                        out=YNEW[:, do % 2, :], in0=PS[pslot][:, :], scalar=0.0,
                        in1=YT[:, do, 1:513], op0=ALU.add, op1=ALU.add),
                        ["PS%d" % pslot, "YT"], ["YN%d" % (do % 2)])
                    store_res(yH, qt, do, do % 2, "y")
            B.phase()

        def inner_ffn(l):
            a = [0]

            def al(n):
                r = A16[:, a[0]:a[0] + n]
                a[0] += n
                return r

            B.phase()
            W1B = al(8 * DI2).rearrange("p (d n) -> p d n", n=DI2)
            W2B = al(22 * 1024).rearrange("p (d n) -> p d n", n=1024)
            ZQ = al(8 * 512).rearrange("p (c n) -> p c n", n=512)
            HB = al(22 * 512).rearrange("p (c n) -> p c n", n=512)
            SQP = al(2 * 512).rearrange("p (b n) -> p b n", n=512)
            RSB = al(512)
            MSB = al(512)
            assert a[0] <= A16_ELEMS, a[0]
            for d2 in range(8):
                B.dma("gp", W1B[:, d2, :], dt["f_w1"][l, d2 * 128:(d2 + 1) * 128, :], [], ["W1B"])
            for j, (o, sz) in enumerate(FCH):
                B.dma("gp", W2B[0:sz, j, :], dt["f_w2"][l, o:o + sz, :], [], ["W2B"])
            for qt in range(NT):
                load_ytile(yH, qt, "y", True)
                B.op("dve", lambda e: e.tensor_copy(out=ZQ[:, 0:4, :], in_=YT[:, 0:4, 1:513]),
                     ["YT"], ["ZQ"])
                B.op("dve", lambda e: e.tensor_copy(out=ZQ[:, 4:8, :], in_=YT[:, 4:8, 0:512]),
                     ["YT", "YTc"], ["ZQ"])
                ln_stats(ZQ, ["ZQ"], SQP, RSB[0:1, :], MSB[0:1, :], ["r.rsb", "r.msb"])
                for c in range(8):
                    sh = slice(1, 513) if c < 4 else slice(0, 512)
                    ytk = ["YT"] if c < 4 else ["YT", "YTc"]
                    B.op("dve", lambda e, c=c, sh=sh: e.scalar_tensor_tensor(
                        out=TM1, in0=YT[:, c, sh], scalar=1.0, in1=PS[6][:, :],
                        op0=ALU.mult, op1=ALU.mult), ytk + ["PS6"], ["TM1"])
                    B.op("dve", lambda e: e.tensor_add(TM1, TM1, PS[7][:, :]), ["TM1", "PS7"], ["TM1"])
                    B.op("dve", lambda e, c=c: e.tensor_scalar(
                        out=ZQ[:, c, :], in0=TM1, scalar1=cf("lng", l * 8 + c), scalar2=cf("lnb", l * 8 + c),
                        op0=ALU.mult, op1=ALU.add), ["TM1", "c.lng", "c.lnb"], ["ZQ"])
                for j, (o, sz) in enumerate(FCH):
                    pa = j % 2
                    pg = 2 + j % 2
                    for d2 in range(8):
                        B.mm(PS[pa][0:sz, :], W1B[:, d2, o:o + sz], ZQ[:, d2, :], d2 == 0, d2 == 7,
                             ["W1B", "ZQ"], ["PS%d" % pa])
                    for d2 in range(8):
                        B.mm(PS[pg][0:sz, :], W1B[:, d2, DI + o:DI + o + sz], ZQ[:, d2, :], d2 == 0, d2 == 7,
                             ["W1B", "ZQ"], ["PS%d" % pg])
                    B.op("act", lambda e, pg=pg, sz=sz, j=j: e.activation(
                        out=HB[0:sz, j, :], in_=PS[pg][0:sz, :], func=AF.Gelu,
                        bias=CONF[0:sz, fmap["b1"] + l * 44 + 22 + j:fmap["b1"] + l * 44 + 23 + j]),
                        ["PS%d" % pg, "c.b1"], ["HB.%d" % j])
                    B.op("dve", lambda e, pa=pa, sz=sz, j=j: e.scalar_tensor_tensor(
                        out=HB[0:sz, j, :], in0=PS[pa][0:sz, :],
                        scalar=CONF[0:sz, fmap["b1"] + l * 44 + j:fmap["b1"] + l * 44 + j + 1],
                        in1=HB[0:sz, j, :], op0=ALU.add, op1=ALU.mult),
                        ["PS%d" % pa, "HB.%d" % j, "c.b1"], ["HB.%d" % j])
                for do in range(8):
                    pslot = 4 + do % 2
                    for j, (o, sz) in enumerate(FCH):
                        B.mm(PS[pslot][:, :], W2B[0:sz, j, do * 128:(do + 1) * 128], HB[0:sz, j, :],
                             j == 0, j == 21, ["W2B", "HB.%d" % j], ["PS%d" % pslot])
                    B.op("dve", lambda e, pslot=pslot, do=do: e.scalar_tensor_tensor(
                        out=YNEW[:, do % 2, :], in0=PS[pslot][:, :], scalar=cf("b2", l * 8 + do),
                        in1=YT[:, do, 1:513], op0=ALU.add, op1=ALU.add),
                        ["PS%d" % pslot, "YT", "c.b2"], ["YN%d" % (do % 2)])
                    store_res(yH, qt, do, do % 2, "y")
            B.phase()

        def dump(src):
            B.phase()
            B.dma("sp", dt["out"], src, [], ["dump"])
            B.phase()

        stage = DEBUG_STAGE
        done = False
        def fin(s_):
            return stage is not None and stage == s_
        outer_attn("a", xT, "hx", xH, "x")
        if fin("attn_a"):
            dump(xH); done = True
        if not done:
            ln_pass(xH, "x")
            if fin("ln0"):
                dump(yH); done = True
        if not done:
            for l in range(DEPTH):
                inner_attn(l)
                if fin("ia%d" % l):
                    dump(yH); done = True; break
                inner_ffn(l)
                if fin("if%d" % l):
                    dump(yH); done = True; break
        if not done:
            final_add(xH, "x")
            if fin("fa0"):
                dump(xH); done = True
        if not done:
            outer_attn("b", xH, "x", xH, "x")
            if fin("attn_b"):
                dump(xH); done = True
        if not done:
            ln_pass(xH, "x")
            for l in range(DEPTH):
                inner_attn(l)
                inner_ffn(l)
            final_add(dt["out"], "o")

        B.finalize(sems)
        import sys
        print("op counts:", {k: len(v) for k, v in B.q.items()}, file=sys.stderr)
        with nc.Block() as block:
            @block.tensor
            def _(e):
                B.replay("pe", e)

            @block.scalar
            def _(e):
                B.replay("act", e)

            @block.vector
            def _(e):
                B.replay("dve", e)

            @block.gpsimd
            def _(e):
                B.replay("gp", e)

            @block.sync
            def _(e):
                B.replay("sp", e)
                e.wait_ge(sems["sp"], 16 * len(B.q["sp"]))
                e.wait_ge(sems["gp"], 16 * len(B.q["gp"]))
    return nc


def kernel(**inputs):
    cons = _host_consts()
    base = {}
    base["aa_wq"] = inputs["L0_attn_a_wq"]
    base["aa_wk"] = inputs["L0_attn_a_wk"]
    base["aa_wv"] = inputs["L0_attn_a_wv"]
    base["aa_wo"] = inputs["L0_attn_a_wo_w"]
    base["aa_wob"] = np.ascontiguousarray(np.asarray(inputs["L0_attn_a_wo_b"]).reshape(8, 128).T)
    base["ab_wq"] = np.ascontiguousarray(np.asarray(inputs["L0_attn_b_Wq"]).T)
    base["ab_wk"] = np.ascontiguousarray(np.asarray(inputs["L0_attn_b_Wk"]).T)
    base["ab_wv"] = np.ascontiguousarray(np.asarray(inputs["L0_attn_b_Wv"]).T)
    base["ab_wo"] = np.ascontiguousarray(np.asarray(inputs["L0_attn_b_Wff"]).T)
    base["ab_bff"] = np.ascontiguousarray(np.asarray(inputs["L0_attn_b_bff"]).reshape(8, 128).T)
    base["f_wqkv"] = inputs["L0_ffn_Wqkv"]
    base["f_wo"] = inputs["L0_ffn_Wo"]
    base["f_w1"] = inputs["L0_ffn_W1"]
    base["f_w2"] = inputs["L0_ffn_W2"]
    qsc = np.asarray(inputs["L0_ffn_q_scale"])
    ksc = np.asarray(inputs["L0_ffn_k_scale"])
    base["f_qs"] = np.ascontiguousarray(np.stack([np.tile(qsc[i], 2) for i in range(DEPTH)], 1))
    base["f_ks"] = np.ascontiguousarray(np.stack([np.tile(ksc[i], 2) for i in range(DEPTH)], 1))
    base["f_lng"] = np.ascontiguousarray(
        np.asarray(inputs["L0_ffn_ln_g"]).reshape(DEPTH, 8, 128).transpose(2, 0, 1).reshape(128, DEPTH * 8))
    base["f_lnb"] = np.ascontiguousarray(
        np.asarray(inputs["L0_ffn_ln_b"]).reshape(DEPTH, 8, 128).transpose(2, 0, 1).reshape(128, DEPTH * 8))
    b1i = np.asarray(inputs["L0_ffn_b1"])
    b1 = np.zeros((128, DEPTH, 44), np.float32)
    for i in range(DEPTH):
        for j, (o, sz) in enumerate(FCH):
            b1[0:sz, i, j] = b1i[i, o:o + sz]
            b1[0:sz, i, 22 + j] = b1i[i, DI + o:DI + o + sz]
    base["f_b1"] = np.ascontiguousarray(b1.reshape(128, DEPTH * 44))
    base["f_b2"] = np.ascontiguousarray(
        np.asarray(inputs["L0_ffn_b2"]).reshape(DEPTH, 8, 128).transpose(2, 0, 1).reshape(128, DEPTH * 8))
    for k, v in cons.items():
        base[k] = v
    for k in base:
        a = np.asarray(base[k])
        if a.dtype == np.float64:
            a = a.astype(np.float32)
        base[k] = np.ascontiguousarray(a)

    nc = build_program()
    xin = np.asarray(inputs["L0_attn_a_x"])
    in_maps = []
    for c in range(8):
        m = dict(base)
        m["xT"] = np.ascontiguousarray(xin[c].T)
        in_maps.append(m)
    res = run_bass_kernel_spmd(nc, in_maps, core_ids=list(range(8)))
    out = np.stack([np.asarray(r["out"], dtype=np.float32).T for r in res.results], 0)
    return np.ascontiguousarray(out)
